# version 03 (final)
# baseline (speedup 1.0000x reference)
import numpy as np
import ml_dtypes
from collections import defaultdict
import concourse.bass as bass
import concourse.mybir as mybir
from concourse.bass_utils import run_bass_kernel_spmd

F32 = mybir.dt.float32
BF16 = mybir.dt.bfloat16
AF = mybir.ActivationFunctionType
ALU = mybir.AluOpType
BF = ml_dtypes.bfloat16

B, S, D = 4, 4096, 2048
NCORE = 8
TOWN = 2048
T = 1024
NPASS = 2
DC = D // 128
H, KVH, DH = 16, 4, 128
CH = 128
NCH = T // CH
E_KV = KVH * DH
NEXP = 16
I = 1024
IC = I // 128
QI = 256
NTH = T // 512
LAM = np.array([2.0 ** (-(h + 1) / 2.0) for h in range(H)])
EPS_LN = 1e-5
EPS_TOPK = 1e-9

USABLE = 212736
DBG = False
SKIP_PREV = False
STOP_AFTER = None
LN_DUMP = True
LN_DELAY = False
STRIP = set()
MOE_DBG = False


class _Stop(Exception):
    pass


class Sched:

    def __init__(self):
        self.items = []
        self.count = defaultdict(int)
        self.prod = {}
        self.waited = defaultdict(lambda: defaultdict(int))
        self.sems = {}

    def channel(self, name, producer):
        self.prod[name] = producer

    def ins(self, eng, fn, inc=None):
        if inc is not None:
            k = 16 if self.prod[inc] == "dma" else 1
            self.count[inc] += k
            self.items.append((eng, lambda E, f=fn, s=inc, kk=k: f(E).then_inc(self.sems[s], kk)))
            return self.count[inc]
        self.items.append((eng, fn))
        return None

    def wait(self, eng, chan, val):
        if val is None or val <= 0:
            return
        if self.prod[chan] == eng:
            return
        if self.waited[eng][chan] >= val:
            return
        self.waited[eng][chan] = val
        self.items.append((eng, lambda E, c=chan, v=val: E.wait_ge(self.sems[c], v)))

    def val(self, chan):
        return self.count[chan]

    def run(self, eng, E):
        for e, fn in self.items:
            if e == eng:
                fn(E)


def _rope_tables(pos0, n):
    inv = (10000.0 ** (-(np.arange(64) / 64.0))).astype(np.float64)
    pos = np.arange(pos0, pos0 + n, dtype=np.float64)
    ang = pos[None, :] * np.concatenate([inv, inv])[:, None]
    return np.cos(ang), np.sin(ang)


def make_tables(par):
    out = {}
    c, s = _rope_tables(par * TOWN, TOWN)
    sf = s.copy()
    sf[0:64, :] *= -1.0
    out["cfm"] = c
    out["sfm"] = sf
    out["ctm"] = c.T.copy()
    out["stm"] = s.T.copy()
    cp, sp = _rope_tables(0, TOWN)
    out["ctm_p"] = cp.T.copy()
    out["stm_p"] = sp.T.copy()
    j = np.arange(CH, dtype=np.float64)
    i = np.arange(CH, dtype=np.float64)
    diag = np.zeros((CH, H * CH))
    qdec = np.zeros((128, H * CH))
    kdec = np.zeros((128, H * CH))
    blkt = np.zeros((128, H * 128))
    for h in range(H):
        dm = np.exp(-LAM[h] * (i[None, :] - j[:, None]))
        dm[i[None, :] < j[:, None]] = 0.0
        diag[:, h * CH:(h + 1) * CH] = dm
        qdec[:, h * CH:(h + 1) * CH] = np.exp(-LAM[h] * (i + 1.0))[None, :]
        kdec[:, h * CH:(h + 1) * CH] = np.exp(-LAM[h] * (CH - 1.0 - j))[:, None]
        blkt[:, h * 128:(h + 1) * 128] = np.exp(-LAM[h] * CH)
    out.update(diag=diag, qdec=qdec, kdec=kdec, blkt=blkt)
    out["ident"] = np.eye(128)
    out["sel16"] = np.repeat(np.eye(NEXP), 128, axis=1)
    return {k: np.ascontiguousarray(v.astype(BF)) for k, v in out.items()}


TAB_SHAPES = dict(cfm=[128, TOWN], sfm=[128, TOWN], ctm=[TOWN, 128], stm=[TOWN, 128],
                  ctm_p=[TOWN, 128], stm_p=[TOWN, 128], diag=[CH, H * CH],
                  qdec=[128, H * CH], kdec=[128, H * CH], blkt=[128, H * 128],
                  ident=[128, 128], sel16=[NEXP, NEXP * 128])


def build_program():
    nc = bass.Bass("TRN2", target_bir_lowering=False)
    dbg_h = {}
    if DBG:
        for nm, shp, dt in [("xn", [128, DC * T], BF16), ("mean", [128, T], BF16),
                            ("rstd", [128, T], BF16), ("probs", [128, 128], F32),
                            ("wg", [16, 8 * 128], BF16), ("q", [128, H * T], BF16),
                            ("kfm", [128, KVH * T], BF16), ("ktm", [128, NCH * 512], BF16),
                            ("vtm", [128, NCH * 512], BF16), ("o", [128, H * T], BF16),
                            ("kv", [128, H * 128], F32), ("kvprev", [128, H * 128], F32),
                            ("accw", [128, DC * T], F32), ("accm", [128, DC * T], F32),
                            ("sq", [128, T], F32), ("sum", [128, T], F32),
                            ("bca0", [128, T], BF16), ("bca1", [128, T], BF16),
                            ("hid0", [128, IC * T], BF16), ("hid1", [128, IC * T], BF16),
                            ("acce0", [128, DC * T], F32), ("acce1", [128, DC * T], F32),
                            ("xnw", [128, DC * T], BF16), ("xnm", [128, DC * T], BF16)]:
            dbg_h[nm] = nc.dram_tensor("dbg_" + nm, shp, dt, kind="ExternalOutput").ap()
    xT_h = nc.dram_tensor("xT", [D, TOWN], F32, kind="ExternalInput").ap()
    xN_h = nc.dram_tensor("xN", [TOWN, D], F32, kind="ExternalInput").ap()
    xpN_h = nc.dram_tensor("xpN", [TOWN, D], F32, kind="ExternalInput").ap()
    wqT_h = nc.dram_tensor("wqT", [D, D], F32, kind="ExternalInput").ap()
    wkT_h = nc.dram_tensor("wkT", [D, E_KV], F32, kind="ExternalInput").ap()
    wvT_h = nc.dram_tensor("wvT", [D, E_KV], F32, kind="ExternalInput").ap()
    woT_h = nc.dram_tensor("woT", [D, D], F32, kind="ExternalInput").ap()
    wrT_h = nc.dram_tensor("wrT", [D, NEXP], F32, kind="ExternalInput").ap()
    bias_h = nc.dram_tensor("bias", [1, NEXP], F32, kind="ExternalInput").ap()
    wg_h = nc.dram_tensor("Wg", [NEXP, D, I], F32, kind="ExternalInput").ap()
    wu_h = nc.dram_tensor("Wu", [NEXP, D, I], F32, kind="ExternalInput").ap()
    wd_h = nc.dram_tensor("Wd", [NEXP, I, D], F32, kind="ExternalInput").ap()
    swg_h = nc.dram_tensor("sWg", [D, I], F32, kind="ExternalInput").ap()
    swu_h = nc.dram_tensor("sWu", [D, I], F32, kind="ExternalInput").ap()
    swd_h = nc.dram_tensor("sWd", [I, D], F32, kind="ExternalInput").ap()
    tabs_h = {nm: nc.dram_tensor("tab_" + nm, shp, BF16, kind="ExternalInput").ap()
              for nm, shp in TAB_SHAPES.items()}
    out_h = nc.dram_tensor("outT", [D, TOWN], F32, kind="ExternalOutput").ap()

    from contextlib import ExitStack
    with ExitStack() as ctx:
        En_ = ctx.enter_context
        arena = En_(nc.sbuf_tensor("arena", [128, USABLE], mybir.dt.uint8))
        ps = En_(nc.psum_tensor("ps", [128, 4096], F32))

        def reg(off, nrow, nelem, dt):
            nb = nelem * (2 if dt == BF16 else 4)
            assert off % 32 == 0 and off + nb <= USABLE, (off, nb)
            return arena[0:nrow, off:off + nb].bitcast(dt)

        xn = reg(0, 128, DC * T, BF16)
        acc = reg(32768, 128, DC * T, F32)
        KV32 = reg(98304, 128, H * 128, F32)
        ra = reg(106496, 128, 512, BF16)
        sel16_sb = reg(106496, 16, NEXP * 128, BF16)
        rb = reg(107520, 128, 512, BF16)
        rr = reg(108544, 128, 512, BF16)
        rs = reg(109568, 128, 512, BF16)
        logits = reg(110592, 128, 128, F32)
        probs = reg(111104, 128, 128, F32)
        pm = reg(111616, 128, 128, F32)
        masks = reg(112128, 128, 128, F32)
        r8a = reg(112640, 128, 8, F32)
        r8b = reg(112672, 128, 8, F32)
        r8c = reg(112704, 128, 8, F32)
        wgtm_bf = reg(112736, 128, 128, BF16)
        bias_bc = reg(112992, 128, 16, F32)
        ident = reg(113056, 128, 128, BF16)
        ones128 = reg(113312, 128, 128, BF16)
        ones1 = reg(113568, 1, 128, BF16)
        epst = reg(120480, 128, 8, F32)
        wrow = [reg(113824, 1, T, BF16), reg(115872, 1, T, BF16)]
        wg_fm = reg(117920, 16, 8 * 128, BF16)
        wrT_sb = reg(119968, 128, DC * 16, BF16)
        P1 = 120512
        o_p = reg(P1, 128, H * T, BF16)
        k_fm = reg(153280, 128, KVH * T, BF16)
        k_tm = reg(161472, 128, NCH * 512, BF16)
        v_tm = reg(169664, 128, NCH * 512, BF16)
        KV16 = reg(177856, 128, H * 128, BF16)
        t_diag = reg(181952, CH, H * CH, BF16)
        t_qdec = reg(186048, 128, H * CH, BF16)
        t_kdec = reg(190144, 128, H * CH, BF16)
        t_blkt = reg(194240, 128, H * 128, BF16)
        sTm = reg(198336, 128, H * CH, BF16)
        q_s = reg(202432, 128, H * CH, BF16)
        k_s = reg(206528, 128, H * 128, BF16)
        xtm = [reg(153280, 128, D, F32), reg(161472, 128, D, F32)]
        xntm = [reg(169664, 128, D, BF16), reg(173760, 128, D, BF16)]
        bnst = [reg(120512, 128, 24, F32), reg(120608, 128, 24, F32)]
        bnag = [reg(120704, 128, 2, F32), reg(120736, 128, 2, F32)]
        lnt = reg(120768, 128, 1, F32)
        rstd1 = [reg(120800, 128, 1, F32), reg(120832, 128, 1, F32)]
        nm_a = [reg(120864, 128, 1, F32), reg(120896, 128, 1, F32)]
        r8d = reg(120928, 128, 8, F32)
        r8e = reg(120960, 128, 8, F32)
        r8f = reg(120992, 128, 8, F32)
        fence8s = reg(210944, 1, 8, BF16)
        mhalf = reg(210976, 128, 1, F32)
        minus1 = reg(211008, 128, 1, F32)
        onesf = reg(211264, 128, 1, F32)
        zerof = reg(211296, 128, 1, F32)
        etopk = reg(211328, 128, 1, F32)
        q_p = reg(32768, 128, H * T, BF16)
        wbuf = reg(65536, 128, DC * E_KV, BF16)
        t_cfm = reg(81920, 128, T, BF16)
        t_sfm = reg(83968, 128, T, BF16)
        t_ctm = reg(86016, 128, NCH * 128, BF16)
        t_stm = reg(88064, 128, NCH * 128, BF16)
        wo_sb = [reg(153280, 128, DC * 256, BF16), reg(161472, 128, DC * 256, BF16)]
        wd_sb = reg(P1, 128, IC * D, BF16)
        wgq = [reg(153280, 128, DC * QI, BF16), reg(161472, 128, DC * QI, BF16)]
        wuq = [reg(169664, 128, DC * QI, BF16), reg(177856 + 4096, 128, DC * QI, BF16)]
        hid = reg(190144, 128, IC * T, BF16)
        bca = [reg(206528, 128, T, BF16), reg(208576, 128, T, BF16)]
        silu_g = [reg(210624, 128, 512, BF16), reg(211648, 128, 512, BF16)]
        silu_s = [reg(177856, 128, 512, BF16), reg(178880, 128, 512, BF16)]
        ob = [reg(P1, 128, T, F32), reg(P1 + 4096, 128, T, F32)]
        xf = [reg(P1 + 8192, 128, T, F32), reg(P1 + 12288, 128, T, F32)]

        S = Sched()
        for chn, pr in [("ld", "dma"), ("ldw", "dma"), ("st", "dma"), ("row", "dma"),
                        ("tp", "dma"),
                        ("pe", "tensor"), ("ve", "vector"), ("se", "scalar"), ("gp", "gpsimd")]:
            S.channel(chn, pr)
            S.sems[chn] = En_(nc.semaphore("sem_" + chn))
        with nc.Block() as blk_init:
            @blk_init.sync
            def _(s):
                for chn in S.sems:
                    s.sem_clear(S.sems[chn])
        blk = En_(nc.Block())

        def pb(i, w=512):
            return ps[:, 512 * i: 512 * i + w]

        def cast_load(dst_ap, src_ap, inc="ldw"):
            S.ins("gpsimd", lambda g, d=dst_ap, s=src_ap: g.dma_start(out=d, in_=s), inc=inc)
            return S.val(inc)

        def barrier():
            snap = {c: S.val(c) for c in ("pe", "ve", "se", "gp", "ld", "ldw", "st", "row")}
            for eng in ("tensor", "vector", "scalar", "gpsimd", "sync"):
                for c, v in snap.items():
                    S.wait(eng, c, v)

        def dump(nm, src_ap):
            if not DBG:
                return
            for c in ("pe", "ve", "se", "gp"):
                S.wait("sync", c, S.val(c))
            S.ins("sync", lambda s, nm=nm, a=src_ap: s.dma_start(out=dbg_h[nm], in_=a), inc="st")

        def ln_consts():
            S.ins("vector", lambda v: v.memset(epst[:], EPS_LN))
            S.ins("vector", lambda v: v.memset(mhalf[:], -0.5))
            S.ins("vector", lambda v: v.memset(minus1[:], -1.0))
            S.ins("vector", lambda v: v.memset(onesf[:], 1.0))
            S.ins("vector", lambda v: v.memset(etopk[:], EPS_TOPK))
            S.ins("vector", lambda v: v.memset(zerof[:], 0.0), inc="ve")

        S.ins("vector", lambda v: v.memset(ones128[:], 1.0))
        S.ins("vector", lambda v: v.memset(ones1[:], 1.0))
        S.ins("vector", lambda v: v.memset(KV32[:], 0.0), inc="ve")
        kv_zero = S.val("ve")
        if "tabs" not in STRIP:
            for nm, dst in [("diag", t_diag), ("qdec", t_qdec), ("kdec", t_kdec),
                            ("blkt", t_blkt), ("ident", ident)]:
                S.ins("sync", lambda s, nm=nm, dst=dst: s.dma_start(out=dst, in_=tabs_h[nm]), inc="ld")
            S.ins("sync", lambda s: s.dma_start(
                out=t_cfm, in_=tabs_h["cfm"][:, 0:T]), inc="ld")
        scan_tabs_done = S.val("ld")
        st0_tabs_marker = scan_tabs_done
        wr_done = (cast_load(wrT_sb.rearrange("p (c e) -> p c e", c=DC),
                             wrT_h.rearrange("(c p) e -> p c e", p=128))
                   if "wr" not in STRIP else S.val("ldw"))
        if "bias" not in STRIP:
            biasb_done = cast_load(rr[0:1, 0:16], bias_h)
            S.wait("tensor", "ldw", biasb_done)
            S.wait("tensor", "ve", kv_zero)
            S.ins("tensor", lambda t: t.matmul(ps[:, 7 * 512: 7 * 512 + 16], ones1[:],
                                               rr[0:1, 0:16], start=True, stop=True), inc="pe")
            pe_bias = S.val("pe")
            S.wait("scalar", "pe", pe_bias)
            S.ins("scalar", lambda sc: sc.copy(bias_bc, ps[:, 7 * 512: 7 * 512 + 16]), inc="se")
        bias_done = S.val("se")

        st_ = dict(proj_bank=[0, 0], kraw=0, kvb=0, ob_=0, sTmf=0, qsf=0, ksf=0,
                   kvadd=0, kv16=0, tabs=0)
        vtm_ready = {}
        ktm_ready = {}
        st_["tabs"] = scan_tabs_done

        def ln(x_nat, tok_off):
            st_.setdefault("lnx_free", [0, 0])
            st_.setdefault("lnt_free", [0, 0])
            st_.setdefault("lnag_free", [0, 0])
            xld_of = {}

            def stage_a(ts):
                b = ts % 2
                S.wait("gpsimd", "ve", st_["lnx_free"][b])
                xld_of[ts] = cast_load(
                    xtm[b], x_nat[tok_off + ts * 128: tok_off + (ts + 1) * 128, :])
                S.wait("vector", "ldw", xld_of[ts])
                for q4 in range(4):
                    S.ins("vector", lambda v, b=b, q4=q4: v.bn_stats(
                        bnst[b][:, q4 * 6:(q4 + 1) * 6], xtm[b][:, q4 * 512:(q4 + 1) * 512]))

            def stage_b(ts):
                b = ts % 2
                S.wait("vector", "ve", st_["lnag_free"][b])
                S.ins("vector", lambda v, b=b: v.bn_aggr(bnag[b], bnst[b]), inc="ve")
                ve_ag = S.val("ve")
                S.wait("scalar", "ve", ve_ag)
                S.ins("scalar", lambda sc, b=b: sc.activation(
                    lnt, bnag[b][:, 1:2], AF.Ln, bias=epst[:, 0:1]))
                S.ins("scalar", lambda sc, b=b: sc.activation(
                    rstd1[b], lnt, AF.Exp, scale=mhalf[:, 0:1]))
                S.ins("scalar", lambda sc, b=b: sc.activation(
                    nm_a[b], bnag[b][:, 0:1], AF.Identity,
                    scale=minus1[:, 0:1], bias=zerof[:, 0:1]), inc="se")
                se_r = S.val("se")
                S.wait("vector", "se", se_r)
                S.ins("vector", lambda v, b=b: v.tensor_tensor(
                    xtm[b], xtm[b], nm_a[b][:, 0:1].broadcast_to((128, D)), ALU.add))
                S.wait("vector", "tp", st_["lnt_free"][b])
                S.ins("vector", lambda v, b=b: v.tensor_scalar(
                    xntm[b], xtm[b], rstd1[b][:, 0:1], None, ALU.mult), inc="ve")
                st_["lnx_free"][b] = S.val("ve")
                st_["lnag_free"][b] = S.val("ve")
                se_x = S.val("ve")
                S.wait("sync", "ve", se_x)
                S.ins("sync", lambda s, b=b, ts=ts: s.dma_start(
                    out=xn.rearrange("p (c t) -> p c t", c=DC)[:, :, ts * 128:(ts + 1) * 128],
                    in_=xntm[b].rearrange("p (c d) -> p c d", c=DC),
                    transpose=True), inc="tp")
                st_["lnt_free"][b] = S.val("tp")

            stage_a(0)
            stage_a(1)
            for ts in range(NCH):
                stage_b(ts)
                if ts + 2 < NCH:
                    stage_a(ts + 2)
            return S.val("tp")

        def router(xn_rdy, pass_idx):
            S.wait("tensor", "tp", xn_rdy)
            S.wait("tensor", "ldw", wr_done)
            for ts in range(NCH):
                for dc in range(DC):
                    S.ins("tensor", lambda t, ts=ts, dc=dc: t.matmul(
                        ps[:, 4 * 512 + ts * 16: 4 * 512 + ts * 16 + 16],
                        xn[:, dc * T + ts * 128: dc * T + ts * 128 + 128],
                        wrT_sb[:, dc * 16:(dc + 1) * 16],
                        start=(dc == 0), stop=(dc == DC - 1)), inc="pe" if dc == DC - 1 else None)
            pe_log = S.val("pe")
            g3 = lambda a: a.rearrange("p (g e) -> p g e", g=NCH)
            bc8 = lambda a: a.unsqueeze(2).broadcast_to([128, NCH, 16])
            bcb = lambda a: a.unsqueeze(1).broadcast_to([128, NCH, 16])
            S.wait("scalar", "pe", pe_log)
            S.ins("scalar", lambda sc: sc.copy(pm, ps[:, 4 * 512: 4 * 512 + 128]), inc="se")
            se_lg = S.val("se")
            S.wait("vector", "se", max(se_lg, bias_done))
            S.ins("vector", lambda v: v.tensor_tensor(
                g3(logits), g3(pm), bcb(bias_bc), ALU.add))
            S.ins("vector", lambda v: v.reduce_max(r8a, g3(logits), axis=mybir.AxisListType.X),
                  inc="ve")
            S.wait("scalar", "ve", S.val("ve"))
            S.ins("scalar", lambda sc: sc.copy(r8d, r8a), inc="se")
            S.wait("vector", "se", S.val("se"))
            S.ins("vector", lambda v: v.tensor_tensor(g3(logits), g3(logits), bc8(r8d), ALU.subtract),
                  inc="ve")
            ve_l = S.val("ve")
            S.wait("scalar", "ve", ve_l)
            S.ins("scalar", lambda sc: sc.activation(probs, logits, AF.Exp), inc="se")
            se_e = S.val("se")
            S.wait("vector", "se", se_e)
            S.ins("vector", lambda v: v.reduce_sum(r8b, g3(probs), axis=mybir.AxisListType.X),
                  inc="ve")
            ve_rs = S.val("ve")
            S.wait("scalar", "ve", ve_rs)
            S.ins("scalar", lambda sc: sc.activation(pm[:, 0:8], r8b, AF.Ln,
                                                     bias=zerof[:, 0:1]))
            S.ins("scalar", lambda sc: sc.activation(r8c, pm[:, 0:8], AF.Exp,
                                                     scale=minus1[:, 0:1]), inc="se")
            se_rs = S.val("se")
            S.wait("vector", "se", se_rs)
            S.ins("vector", lambda v: v.tensor_tensor(g3(probs), g3(probs), bc8(r8c), ALU.mult))
            S.ins("vector", lambda v: v.reduce_max(r8a, g3(probs), axis=mybir.AxisListType.X),
                  inc="ve")
            S.wait("scalar", "ve", S.val("ve"))
            S.ins("scalar", lambda sc: sc.copy(r8e, r8a), inc="se")
            S.wait("vector", "se", S.val("se"))
            S.ins("vector", lambda v: v.tensor_tensor(g3(masks), g3(probs), bc8(r8e), ALU.is_ge))
            S.ins("vector", lambda v: v.tensor_mul(pm, probs, masks))
            S.ins("vector", lambda v: v.tensor_sub(pm, probs, pm))
            S.ins("vector", lambda v: v.reduce_max(r8b, g3(pm), axis=mybir.AxisListType.X),
                  inc="ve")
            S.wait("scalar", "ve", S.val("ve"))
            S.ins("scalar", lambda sc: sc.copy(r8f, r8b), inc="se")
            S.wait("vector", "se", S.val("se"))
            S.ins("vector", lambda v: v.tensor_tensor(g3(pm), g3(pm), bc8(r8f), ALU.is_ge))
            S.ins("vector", lambda v: v.tensor_add(masks, masks, pm))
            S.ins("vector", lambda v: v.tensor_add(r8c, r8e, r8f), inc="ve")
            ve_ws = S.val("ve")
            S.wait("scalar", "ve", ve_ws)
            S.ins("scalar", lambda sc: sc.activation(pm[:, 0:8], r8c, AF.Ln,
                                                     bias=etopk[:, 0:1]))
            S.ins("scalar", lambda sc: sc.activation(r8c, pm[:, 0:8], AF.Exp,
                                                     scale=minus1[:, 0:1]), inc="se")
            se_ws = S.val("se")
            S.wait("vector", "se", se_ws)
            S.ins("vector", lambda v: v.tensor_mul(probs, probs, masks))
            S.ins("vector", lambda v: v.tensor_tensor(g3(probs), g3(probs), bc8(r8c), ALU.mult),
                  inc="ve")
            S.wait("scalar", "ve", S.val("ve"))
            S.ins("scalar", lambda sc: sc.copy(wgtm_bf, probs), inc="se")
            se_w = S.val("se")
            S.wait("tensor", "se", se_w)
            S.wait("tensor", "ld", scan_tabs_done)
            for ts in range(NCH):
                S.ins("tensor", lambda t, ts=ts: t.matmul(
                    ps[0:16, 5 * 512 + ts * 128: 5 * 512 + (ts + 1) * 128],
                    wgtm_bf[:, ts * 16:(ts + 1) * 16], ident, start=True, stop=True), inc="pe")
            pe_t = S.val("pe")
            S.wait("scalar", "pe", pe_t)
            S.ins("scalar", lambda sc: sc.copy(
                wg_fm, ps[0:16, 5 * 512: 5 * 512 + 8 * 128]), inc="se")
            return S.val("se")

        def load_wv(guard_pe):
            S.wait("gpsimd", "pe", guard_pe)
            return cast_load(wbuf.rearrange("p (c e) -> p c e", c=DC),
                             wvT_h.rearrange("(c p) e -> p c e", p=128))

        def load_wk(guard_pe):
            S.wait("gpsimd", "pe", guard_pe)
            return cast_load(wbuf.rearrange("p (c e) -> p c e", c=DC),
                             wkT_h.rearrange("(c p) e -> p c e", p=128))

        def proj_v(ts, xn_rdy, w_rdy):
            b = ts % 2
            S.wait("tensor", "tp", xn_rdy)
            S.wait("tensor", "ldw", w_rdy)
            S.wait("tensor", "se", st_["proj_bank"][b])
            for dc in range(DC):
                S.ins("tensor", lambda t, b=b, dc=dc, ts=ts: t.matmul(
                    pb(b), xn[:, dc * T + ts * 128: dc * T + ts * 128 + 128],
                    wbuf[:, dc * E_KV: dc * E_KV + 512],
                    start=(dc == 0), stop=(dc == DC - 1)), inc="pe" if dc == DC - 1 else None)
            S.wait("scalar", "pe", S.val("pe"))
            S.ins("scalar", lambda sc, b=b, ts=ts: sc.copy(
                v_tm[:, ts * 512:(ts + 1) * 512], pb(b)), inc="se")
            st_["proj_bank"][b] = S.val("se")
            vtm_ready[ts] = S.val("se")

        def proj_k_tm(ts, xn_rdy, w_rdy, ct, stab, tab_ld):
            b = ts % 2
            S.wait("tensor", "tp", xn_rdy)
            S.wait("tensor", "ldw", w_rdy)
            S.wait("tensor", "se", st_["proj_bank"][b])
            for dc in range(DC):
                S.ins("tensor", lambda t, b=b, dc=dc, ts=ts: t.matmul(
                    pb(b), xn[:, dc * T + ts * 128: dc * T + ts * 128 + 128],
                    wbuf[:, dc * E_KV: dc * E_KV + 512],
                    start=(dc == 0), stop=(dc == DC - 1)), inc="pe" if dc == DC - 1 else None)
            S.wait("scalar", "pe", S.val("pe"))
            S.wait("scalar", "ve", st_["kraw"])
            S.ins("scalar", lambda sc, b=b: sc.copy(rr, pb(b)), inc="se")
            st_["proj_bank"][b] = S.val("se")
            se_k = S.val("se")
            g4 = lambda a: a.rearrange("p (g d) -> p g d", g=KVH)
            S.wait("vector", "se", se_k)
            S.wait("vector", "ld", tab_ld)
            S.ins("vector", lambda v, ct=ct, ts=ts: v.tensor_tensor(
                g4(ra), g4(rr),
                ct[:, ts * 128:(ts + 1) * 128].unsqueeze(1).broadcast_to([128, KVH, 128]), ALU.mult))
            S.ins("vector", lambda v, stab=stab, ts=ts: v.tensor_tensor(
                g4(rb), g4(rr),
                stab[:, ts * 128:(ts + 1) * 128].unsqueeze(1).broadcast_to([128, KVH, 128]), ALU.mult),
                inc="ve")
            st_["kraw"] = S.val("ve")
            S.ins("vector", lambda v, ts=ts: v.tensor_tensor(
                g4(k_tm[:, ts * 512:(ts + 1) * 512])[:, :, 0:64],
                g4(ra)[:, :, 0:64], g4(rb)[:, :, 64:128], ALU.subtract))
            S.ins("vector", lambda v, ts=ts: v.tensor_tensor(
                g4(k_tm[:, ts * 512:(ts + 1) * 512])[:, :, 64:128],
                g4(ra)[:, :, 64:128], g4(rb)[:, :, 0:64], ALU.add), inc="ve")
            ktm_ready[ts] = S.val("ve")

        def rope_fm_store(dst_full, th, tab_ld):
            S.ins("vector", lambda v: v.tensor_copy(rs[0:64, :], rr[64:128, :]))
            S.ins("vector", lambda v: v.tensor_copy(rs[64:128, :], rr[0:64, :]))
            S.wait("vector", "ld", tab_ld)
            S.ins("vector", lambda v, th=th: v.tensor_tensor(
                ra, rr, t_cfm[:, th * 512:(th + 1) * 512], ALU.mult))
            S.ins("vector", lambda v, th=th: v.tensor_tensor(
                rb, rs, t_sfm[:, th * 512:(th + 1) * 512], ALU.mult), inc="ve")
            st_["kraw"] = S.val("ve")
            S.ins("vector", lambda v, d=dst_full: v.tensor_tensor(d, ra, rb, ALU.add), inc="ve")

        def scan_chunk(c, full, kfm_ve=0, q_ve=0):
            if full:
                S.wait("tensor", "ve", max(st_["kvb"], kfm_ve, q_ve))
                for g in range(KVH):
                    S.ins("tensor", lambda t, g=g, c=c: t.matmul(
                        ps[:, g * 512:(g + 1) * 512],
                        k_fm[:, g * T + c * 128: g * T + c * 128 + 128],
                        q_p.rearrange("p (h t) -> p h t", h=H)[:, 4 * g:4 * g + 4, c * 128:(c + 1) * 128],
                        start=True, stop=True), inc="pe")
                pe_sT = S.val("pe")
                S.wait("vector", "pe", max(pe_sT, st_["sTmf"]))
                S.ins("vector", lambda v: v.tensor_tensor(
                    sTm, ps[:, 0:2048], t_diag, ALU.mult), inc="ve")
                ve_mask = S.val("ve")
                S.wait("vector", "pe", st_["qsf"])
                S.ins("vector", lambda v, c=c: v.tensor_tensor(
                    q_s.rearrange("p (h d) -> p h d", h=H),
                    q_p.rearrange("p (h t) -> p h t", h=H)[:, :, c * 128:(c + 1) * 128],
                    t_qdec.rearrange("p (h d) -> p h d", h=H), ALU.mult), inc="ve")
                ve_qs = S.val("ve")
                S.wait("tensor", "ve", max(ve_mask, ve_qs))
                S.wait("tensor", "se", max(st_["ob_"], st_["kv16"], vtm_ready[c]))
                for g in range(KVH):
                    S.ins("tensor", lambda t, g=g, c=c: t.matmul(
                        ps[:, 2048 + g * 512: 2048 + (g + 1) * 512],
                        v_tm[:, c * 512 + g * 128: c * 512 + g * 128 + 128],
                        sTm[:, g * 512:(g + 1) * 512], start=True, stop=False), inc="pe")
                for h in range(H):
                    S.ins("tensor", lambda t, h=h: t.matmul(
                        ps[:, 2048 + h * 128: 2048 + (h + 1) * 128],
                        KV16[:, h * 128:(h + 1) * 128],
                        q_s[:, h * 128:(h + 1) * 128], start=False, stop=True), inc="pe")
                pe_o = S.val("pe")
                st_["sTmf"] = pe_o
                st_["qsf"] = pe_o
                S.wait("scalar", "pe", pe_o)
                S.ins("scalar", lambda sc, c=c: sc.copy(
                    o_p.rearrange("p (h t) -> p h t", h=H)[:, :, c * 128:(c + 1) * 128],
                    ps[:, 2048:4096].rearrange("p (h d) -> p h d", h=H)), inc="se")
                st_["ob_"] = S.val("se")
            S.wait("vector", "pe", st_["ksf"])
            S.ins("vector", lambda v, c=c: v.tensor_tensor(
                k_s.rearrange("p (g r d) -> p g r d", g=KVH, r=KVH),
                k_tm[:, c * 512:(c + 1) * 512].rearrange("p (g d) -> p g d", g=KVH)
                    .unsqueeze(2).broadcast_to([128, KVH, KVH, 128]),
                t_kdec.rearrange("p (g r d) -> p g r d", g=KVH, r=KVH), ALU.mult), inc="ve")
            ks_rdy = S.val("ve")
            S.wait("tensor", "ve", ks_rdy)
            S.wait("tensor", "se", vtm_ready[c])
            for h in range(H):
                g = h // 4
                S.ins("tensor", lambda t, h=h, g=g, c=c: t.matmul(
                    ps[:, h * 128:(h + 1) * 128],
                    k_s[:, h * 128:(h + 1) * 128],
                    v_tm[:, c * 512 + g * 128: c * 512 + g * 128 + 128],
                    start=True, stop=True), inc="pe" if h == H - 1 else None)
            pe_kvd = S.val("pe")
            st_["ksf"] = pe_kvd
            S.wait("gpsimd", "ve", st_["kvadd"] if st_["kvadd"] else kv_zero)
            S.wait("gpsimd", "se", st_["kv16"])
            S.wait("gpsimd", "ld", st_["tabs"])
            S.ins("gpsimd", lambda g: g.tensor_tensor(KV32, KV32, t_blkt, ALU.mult), inc="gp")
            gp_scale = S.val("gp")
            S.wait("vector", "gp", gp_scale)
            S.wait("vector", "pe", pe_kvd)
            S.ins("vector", lambda v: v.tensor_tensor(KV32, ps[:, 0:2048], KV32, ALU.add), inc="ve")
            st_["kvadd"] = S.val("ve")
            st_["kvb"] = st_["kvadd"]
            if full:
                S.wait("scalar", "ve", st_["kvadd"])
                S.ins("scalar", lambda sc: sc.copy(KV16, KV32), inc="se")
                st_["kv16"] = S.val("se")

        for sp in range(0 if SKIP_PREV else 2):
            barrier()
            ln_consts()
            xn_rdy = ln(xpN_h, sp * T)
            S.wait("sync", "ve", st_["kraw"])
            for nm, dst in [("ctm_p", t_ctm), ("stm_p", t_stm)]:
                S.ins("sync", lambda s, nm=nm, dst=dst, sp=sp: s.dma_start(
                    out=dst.rearrange("p (c d) -> p c d", c=NCH),
                    in_=tabs_h[nm][sp * T:(sp + 1) * T].rearrange("(c p) d -> p c d", p=128)),
                    inc="ld")
            tmtab = S.val("ld")
            wv_rdy = load_wv(S.val("pe"))
            for ts in range(NCH):
                proj_v(ts, xn_rdy, wv_rdy)
            wk_rdy = load_wk(S.val("pe"))
            for ts in range(NCH):
                proj_k_tm(ts, xn_rdy, wk_rdy, t_ctm, t_stm, tmtab)
            S.wait("tensor", "se", S.val("se"))
            S.wait("tensor", "ve", S.val("ve"))
            for c in range(NCH):
                scan_chunk(c, full=False)

        if "kvdump" not in STRIP:
            dump("kvprev", KV32)
        wo_done_prev = 0
        def emit_passes():
          for p in range(NPASS):
            barrier()
            ln_consts()
            if "passtabs" not in STRIP:
                for nm, dst in [("cfm", t_cfm), ("sfm", t_sfm)]:
                    S.ins("sync", lambda s, nm=nm, dst=dst, p=p: s.dma_start(
                        out=dst, in_=tabs_h[nm][:, p * T:(p + 1) * T]), inc="ld")
                for nm, dst in [("ctm", t_ctm), ("stm", t_stm)]:
                    S.ins("sync", lambda s, nm=nm, dst=dst, p=p: s.dma_start(
                        out=dst.rearrange("p (c d) -> p c d", c=NCH),
                        in_=tabs_h[nm][p * T:(p + 1) * T].rearrange("(c p) d -> p c d", p=128)), inc="ld")
            tabp = S.val("ld")
            if p > 0:
                for nm, dst in [("diag", t_diag), ("qdec", t_qdec), ("kdec", t_kdec),
                                ("blkt", t_blkt)]:
                    S.ins("sync", lambda s, nm=nm, dst=dst: s.dma_start(out=dst, in_=tabs_h[nm]),
                          inc="ld")
                scan_tabs = S.val("ld")
            else:
                scan_tabs = scan_tabs_done
            xn_rdy = ln(xN_h, p * T)
            if p == 0:
                dump("xn", xn)
                if STOP_AFTER == "ln":
                    raise _Stop
            wg_ve = router(xn_rdy, p)
            if p == 0:
                dump("probs", probs)
                dump("wg", wg_fm)
                if STOP_AFTER == "router":
                    raise _Stop
            wv_rdy = load_wv(S.val("pe"))
            for ts in range(NCH):
                proj_v(ts, xn_rdy, wv_rdy)
            wk_rdy = load_wk(S.val("pe"))
            for ts in range(NCH):
                proj_k_tm(ts, xn_rdy, wk_rdy, t_ctm, t_stm, tabp)
            for g in range(KVH):
                for th in range(NTH):
                    b = th % 2
                    S.wait("tensor", "se", st_["proj_bank"][b])
                    for dc in range(DC):
                        S.ins("tensor", lambda t, b=b, dc=dc, g=g, th=th: t.matmul(
                            pb(b), wbuf[:, dc * E_KV + g * 128: dc * E_KV + g * 128 + 128],
                            xn[:, dc * T + th * 512: dc * T + th * 512 + 512],
                            start=(dc == 0), stop=(dc == DC - 1)), inc="pe" if dc == DC - 1 else None)
                    S.wait("scalar", "pe", S.val("pe"))
                    S.wait("scalar", "ve", st_["kraw"])
                    S.ins("scalar", lambda sc, b=b: sc.copy(rr, pb(b)), inc="se")
                    st_["proj_bank"][b] = S.val("se")
                    S.wait("vector", "se", S.val("se"))
                    rope_fm_store(k_fm[:, g * T + th * 512: g * T + th * 512 + 512], th, tabp)
            kfm_ve = S.val("ve")
            wq_used = 0
            for qq in range(8):
                guard = wq_used if qq else S.val("pe")
                S.wait("gpsimd", "pe", guard)
                wq_rdy = cast_load(wbuf[:, 0:DC * 256].rearrange("p (c e) -> p c e", c=DC),
                                   wqT_h[:, qq * 256:(qq + 1) * 256].rearrange("(c p) e -> p c e", p=128))
                for el in range(2):
                    h = qq * 2 + el
                    for th in range(NTH):
                        b = th % 2
                        S.wait("tensor", "ldw", wq_rdy)
                        S.wait("tensor", "se", st_["proj_bank"][b])
                        for dc in range(DC):
                            S.ins("tensor", lambda t, b=b, dc=dc, el=el, th=th: t.matmul(
                                pb(b), wbuf[:, dc * 256 + el * 128: dc * 256 + el * 128 + 128],
                                xn[:, dc * T + th * 512: dc * T + th * 512 + 512],
                                start=(dc == 0), stop=(dc == DC - 1)), inc="pe" if dc == DC - 1 else None)
                        S.wait("scalar", "pe", S.val("pe"))
                        S.wait("scalar", "ve", st_["kraw"])
                        S.ins("scalar", lambda sc, b=b: sc.copy(rr, pb(b)), inc="se")
                        st_["proj_bank"][b] = S.val("se")
                        S.wait("vector", "se", S.val("se"))
                        rope_fm_store(q_p[:, h * T + th * 512: h * T + th * 512 + 512], th, tabp)
                wq_used = S.val("pe")
            q_ve = S.val("ve")
            if p == 0:
                dump("q", q_p)
                dump("kfm", k_fm)
                dump("ktm", k_tm)
                dump("vtm", v_tm)
                if STOP_AFTER == "proj":
                    raise _Stop
            S.wait("scalar", "ve", st_["kvadd"] if st_["kvadd"] else kv_zero)
            S.ins("scalar", lambda sc: sc.copy(KV16, KV32), inc="se")
            st_["kv16"] = S.val("se")
            st_["tabs"] = scan_tabs
            S.wait("tensor", "se", S.val("se"))
            S.wait("tensor", "ve", S.val("ve"))
            S.wait("tensor", "ld", scan_tabs)
            S.wait("vector", "ld", scan_tabs)
            for c in range(NCH):
                scan_chunk(c, full=True, kfm_ve=kfm_ve, q_ve=q_ve)
            scan_pe = S.val("pe")
            scan_ve = S.val("ve")
            scan_se = S.val("se")
            if p == 0:
                dump("o", o_p)
                dump("kv", KV32)
                if STOP_AFTER == "scan":
                    raise _Stop

            S.wait("sync", "pe", scan_pe)
            S.wait("sync", "ve", scan_ve)
            S.ins("sync", lambda s: s.dma_start(out=sel16_sb, in_=tabs_h["sel16"]), inc="ld")
            sel_rdy = S.val("ld")
            wo_rdy = {}
            wo_used = [0, 0]
            for ee in range(2):
                S.wait("gpsimd", "pe", scan_pe)
                S.wait("gpsimd", "ve", scan_ve)
                wo_rdy[ee] = cast_load(wo_sb[ee % 2].rearrange("p (c e) -> p c e", c=DC),
                                       woT_h[:, ee * 256:(ee + 1) * 256].rearrange("(c p) e -> p c e", p=128))
            wo_bank_free = [0, 0]
            S.wait("tensor", "ve", scan_ve)
            S.wait("tensor", "se", scan_se)
            for ee in range(8):
                buf = ee % 2
                for oc_l in range(2):
                    oc = ee * 2 + oc_l
                    for th in range(NTH):
                        b = th % 2
                        S.wait("tensor", "ldw", wo_rdy[ee])
                        S.wait("tensor", "se", wo_bank_free[b])
                        for ec in range(DC):
                            S.ins("tensor", lambda t, b=b, ec=ec, buf=buf, oc_l=oc_l, th=th: t.matmul(
                                pb(b), wo_sb[buf][:, ec * 256 + oc_l * 128: ec * 256 + oc_l * 128 + 128],
                                o_p[:, ec * T + th * 512: ec * T + th * 512 + 512],
                                start=(ec == 0), stop=(ec == DC - 1)), inc="pe" if ec == DC - 1 else None)
                        pe_wo = S.val("pe")
                        S.wait("scalar", "pe", pe_wo)
                        S.ins("scalar", lambda sc, b=b, oc=oc, th=th: sc.copy(
                            acc[:, oc * T + th * 512: oc * T + th * 512 + 512], pb(b)), inc="se")
                        wo_bank_free[b] = S.val("se")
                if ee + 2 < 8:
                    S.wait("gpsimd", "pe", S.val("pe"))
                    wo_rdy[ee + 2] = cast_load(wo_sb[buf].rearrange("p (c e) -> p c e", c=DC),
                                               woT_h[:, (ee + 2) * 256:(ee + 3) * 256]
                                               .rearrange("(c p) e -> p c e", p=128))
            wo_se = S.val("se")
            S.wait("vector", "se", wo_se)
            S.ins("vector", lambda v: v.tensor_copy(fence8s, fence8s), inc="ve")
            wo_done = S.val("ve")
            if p == 0:
                dump("accw", acc)
                dump("xnw", xn)
                if STOP_AFTER == "wo":
                    raise _Stop

            gu_bank_free = [0, 0]
            u_bank_free = [wo_done, wo_done]
            d_bank_free = [0, 0]
            silu_free = [0, 0]
            wd_buf_free = 0
            bca_free = [0, 0]
            bank67_free = scan_se
            wgu_ready = {}
            wd_ready = {}

            def load_q_ex(e, q):
                buf = q % 2
                if e < NEXP:
                    sg = wg_h[e, :, q * QI:(q + 1) * QI]
                    su = wu_h[e, :, q * QI:(q + 1) * QI]
                else:
                    sg = swg_h[:, q * QI:(q + 1) * QI]
                    su = swu_h[:, q * QI:(q + 1) * QI]
                S.ins("gpsimd", lambda g, sg=sg, buf=buf: g.dma_start(
                    out=wgq[buf].rearrange("p (c i) -> p c i", c=DC),
                    in_=sg.rearrange("(c p) i -> p c i", p=128)), inc="ldw")
                S.ins("gpsimd", lambda g, su=su, buf=buf: g.dma_start(
                    out=wuq[buf].rearrange("p (c i) -> p c i", c=DC),
                    in_=su.rearrange("(c p) i -> p c i", p=128)), inc="ldw")
                return S.val("ldw")

            def load_wd_ex(e):
                half = IC * D // 2
                src = wd_h[e] if e < NEXP else swd_h
                S.ins("gpsimd", lambda g, src=src: g.dma_start(
                    out=wd_sb[:, 0:half].rearrange("p (c h) -> p c h", c=IC // 2),
                    in_=src[0:I // 2, :].rearrange("(c p) h -> p c h", p=128)), inc="ldw")
                S.ins("gpsimd", lambda g, src=src: g.dma_start(
                    out=wd_sb[:, half:].rearrange("p (c h) -> p c h", c=IC // 2),
                    in_=src[I // 2:, :].rearrange("(c p) h -> p c h", p=128)), inc="ldw")
                return S.val("ldw")

            S.wait("gpsimd", "pe", S.val("pe"))
            S.wait("gpsimd", "ve", S.val("ve"))
            wgu_ready[(0, 0)] = load_q_ex(0, 0)
            wgu_ready[(0, 1)] = load_q_ex(0, 1)

            NEALL = NEXP + 1
            for e in range(NEALL):
                par = e % 2
                is_sh = (e == NEXP)
                if not is_sh:
                    S.wait("tensor", "ld", sel_rdy)
                    S.wait("tensor", "se", max(bank67_free, wg_ve))
                    for th in range(NTH):
                        S.ins("tensor", lambda t, e=e, th=th: t.matmul(
                            pb(6 + th), sel16_sb[:, e * 128:(e + 1) * 128],
                            wg_fm[0:16, th * 512:(th + 1) * 512],
                            start=True, stop=True), inc="pe")
                    pe_b = S.val("pe")
                    S.wait("scalar", "pe", pe_b)
                    S.wait("scalar", "ve", bca_free[par])
                    for th in range(NTH):
                        S.ins("scalar", lambda sc, par=par, th=th: sc.copy(
                            bca[par][:, th * 512:(th + 1) * 512], pb(6 + th)), inc="se")
                    bca_ready = S.val("se")
                    bank67_free = S.val("se")
                else:
                    bca_ready = 0
                S.wait("gpsimd", "pe", wd_buf_free if e else S.val("pe"))
                if e == 0:
                    S.wait("gpsimd", "ve", wo_done)
                wd_ready[e] = load_wd_ex(e)
                tile = 0
                for q in range(4):
                    buf = q % 2
                    wq_rdy = wgu_ready[(e, q)]
                    for ic_l in range(2):
                        ic = q * 2 + ic_l
                        for th in range(NTH):
                            bb = tile % 2
                            S.wait("tensor", "ldw", wq_rdy)
                            S.wait("tensor", "se", gu_bank_free[bb])
                            S.wait("tensor", "ve", u_bank_free[bb])
                            for dc in range(DC):
                                S.ins("tensor", lambda t, bb=bb, dc=dc, ic_l=ic_l, th=th, buf=buf: t.matmul(
                                    pb(bb), wgq[buf][:, dc * QI + ic_l * 128: dc * QI + ic_l * 128 + 128],
                                    xn[:, dc * T + th * 512: dc * T + th * 512 + 512],
                                    start=(dc == 0), stop=(dc == DC - 1)),
                                    inc="pe" if dc == DC - 1 else None)
                            pe_g = S.val("pe")
                            for dc in range(DC):
                                S.ins("tensor", lambda t, bb=bb, dc=dc, ic_l=ic_l, th=th, buf=buf: t.matmul(
                                    pb(2 + bb), wuq[buf][:, dc * QI + ic_l * 128: dc * QI + ic_l * 128 + 128],
                                    xn[:, dc * T + th * 512: dc * T + th * 512 + 512],
                                    start=(dc == 0), stop=(dc == DC - 1)),
                                    inc="pe" if dc == DC - 1 else None)
                            pe_u = S.val("pe")
                            S.wait("scalar", "pe", pe_g)
                            S.wait("scalar", "ve", silu_free[bb])
                            S.ins("scalar", lambda sc, bb=bb: sc.activation(
                                silu_g[bb], pb(bb), AF.Silu), inc="se")
                            gu_bank_free[bb] = S.val("se")
                            se_silu = S.val("se")
                            S.wait("vector", "se", max(se_silu, bca_ready))
                            S.wait("vector", "pe", pe_u)
                            if not is_sh:
                                S.ins("vector", lambda v, bb=bb, par=par, th=th: v.tensor_mul(
                                    silu_s[bb], silu_g[bb], bca[par][:, th * 512:(th + 1) * 512]))
                                src_sil = silu_s
                            else:
                                src_sil = silu_g
                            S.ins("vector", lambda v, bb=bb, ic=ic, th=th, src=src_sil: v.tensor_mul(
                                hid[:, ic * T + th * 512: ic * T + th * 512 + 512],
                                src[bb], pb(2 + bb)), inc="ve")
                            u_bank_free[bb] = S.val("ve")
                            silu_free[bb] = S.val("ve")
                            tile += 1
                    pe_after_q = S.val("pe")
                    nxt = (e, q + 2) if q + 2 < 4 else ((e + 1, q - 2) if e + 1 < NEALL else None)
                    if nxt is not None:
                        S.wait("gpsimd", "pe", pe_after_q)
                        wgu_ready[nxt] = load_q_ex(*nxt)

                hid_ready = S.val("ve")
                bca_free[par] = S.val("ve")
                if DBG and MOE_DBG and p == 0 and e < 2:
                    dump("bca%d" % e, bca[par])
                    dump("hid%d" % e, hid)
                dt = 0
                for oc in range(DC):
                    for th in range(NTH):
                        bb = dt % 2
                        S.wait("tensor", "ldw", wd_ready[e])
                        S.wait("tensor", "ve", max(hid_ready, d_bank_free[bb]))
                        for ic in range(IC):
                            S.ins("tensor", lambda t, bb=bb, ic=ic, oc=oc, th=th: t.matmul(
                                pb(4 + bb), wd_sb[:, ic * D + oc * 128: ic * D + oc * 128 + 128],
                                hid[:, ic * T + th * 512: ic * T + th * 512 + 512],
                                start=(ic == 0), stop=(ic == IC - 1)),
                                inc="pe" if ic == IC - 1 else None)
                        pe_d = S.val("pe")
                        S.wait("vector", "pe", pe_d)
                        S.ins("vector", lambda v, bb=bb, oc=oc, th=th: v.tensor_add(
                            acc[:, oc * T + th * 512: oc * T + th * 512 + 512],
                            pb(4 + bb),
                            acc[:, oc * T + th * 512: oc * T + th * 512 + 512]), inc="ve")
                        d_bank_free[bb] = S.val("ve")
                        dt += 1
                wd_buf_free = S.val("pe")
                if DBG and MOE_DBG and p == 0 and e < 2:
                    dump("acce%d" % e, acc)
            moe_ve = S.val("ve")
            if p == 0:
                dump("xnm", xn)
                dump("accm", acc)

            S.wait("sync", "ve", moe_ve)
            xf_ready = {}
            for k in range(2):
                S.ins("sync", lambda s, k=k, p=p: s.dma_start(
                    out=xf[k % 2][:, :], in_=xT_h[k * 128:(k + 1) * 128, p * T:(p + 1) * T]), inc="ld")
                xf_ready[k] = S.val("ld")
            st_base = S.val("st")
            for oc in range(DC):
                S.wait("vector", "ld", xf_ready[oc])
                if oc >= 2:
                    S.wait("vector", "st", st_base + 16 * (oc - 1))
                S.ins("vector", lambda v, oc=oc: v.tensor_add(
                    ob[oc % 2][:, :], acc[:, oc * T:(oc + 1) * T], xf[oc % 2][:, :]), inc="ve")
                ve_ob = S.val("ve")
                S.wait("sync", "ve", ve_ob)
                S.ins("sync", lambda s, oc=oc, p=p: s.dma_start(
                    out=out_h[oc * 128:(oc + 1) * 128, p * T:(p + 1) * T], in_=ob[oc % 2][:, :]),
                    inc="st")
                if oc + 2 < DC:
                    S.wait("sync", "st", st_base + 16 * (oc + 1))
                    S.ins("sync", lambda s, oc=oc, p=p: s.dma_start(
                        out=xf[oc % 2][:, :], in_=xT_h[(oc + 2) * 128:(oc + 3) * 128, p * T:(p + 1) * T]),
                        inc="ld")
                    xf_ready[oc + 2] = S.val("ld")
            S.wait("sync", "st", S.val("st"))

        try:
            emit_passes()
        except _Stop:
            pass
        S.wait("sync", "st", S.val("st"))

        @blk.gpsimd
        def _(g):
            S.run("gpsimd", g)

        @blk.sync
        def _(s):
            S.run("sync", s)

        @blk.tensor
        def _(t):
            S.run("tensor", t)

        @blk.scalar
        def _(sc):
            S.run("scalar", sc)

        @blk.vector
        def _(v):
            S.run("vector", v)

    return nc


def kernel(L0_attn_x, L0_attn_wq, L0_attn_wk, L0_attn_wv, L0_attn_wo,
           L0_ffn_Wr, L0_ffn_bias, L0_ffn_Wg, L0_ffn_Wu, L0_ffn_Wd,
           L0_ffn_sWg, L0_ffn_sWu, L0_ffn_sWd):
    x = np.asarray(L0_attn_x, dtype=np.float32)
    wqT = np.ascontiguousarray(np.asarray(L0_attn_wq, np.float32).T)
    wkT = np.ascontiguousarray(np.asarray(L0_attn_wk, np.float32).T)
    wvT = np.ascontiguousarray(np.asarray(L0_attn_wv, np.float32).T)
    woT = np.ascontiguousarray(np.asarray(L0_attn_wo, np.float32).T)
    wrT = np.ascontiguousarray(np.asarray(L0_ffn_Wr, np.float32).T)
    bias = np.ascontiguousarray(np.asarray(L0_ffn_bias, np.float32).reshape(1, NEXP))
    Wg = np.ascontiguousarray(np.asarray(L0_ffn_Wg, np.float32))
    Wu = np.ascontiguousarray(np.asarray(L0_ffn_Wu, np.float32))
    Wd = np.ascontiguousarray(np.asarray(L0_ffn_Wd, np.float32))
    sWg = np.ascontiguousarray(np.asarray(L0_ffn_sWg, np.float32))
    sWu = np.ascontiguousarray(np.asarray(L0_ffn_sWu, np.float32))
    sWd = np.ascontiguousarray(np.asarray(L0_ffn_sWd, np.float32))

    tabs = [make_tables(0), make_tables(1)]
    zeros_prev = np.zeros((TOWN, D), np.float32)
    in_maps = []
    for c in range(NCORE):
        b, hf = c // 2, c % 2
        xT = np.ascontiguousarray(x[b, hf * TOWN:(hf + 1) * TOWN, :].T)
        xN = np.ascontiguousarray(x[b, hf * TOWN:(hf + 1) * TOWN, :])
        xpN = np.ascontiguousarray(x[b, 0:TOWN, :]) if hf == 1 else zeros_prev
        m = {"xT": xT, "xN": xN, "xpN": xpN, "wqT": wqT, "wkT": wkT, "wvT": wvT, "woT": woT,
             "wrT": wrT, "bias": bias, "Wg": Wg, "Wu": Wu, "Wd": Wd,
             "sWg": sWg, "sWu": sWu, "sWd": sWd}
        for nm in TAB_SHAPES:
            m["tab_" + nm] = tabs[hf][nm]
        in_maps.append(m)

    nc = build_program()
    res = run_bass_kernel_spmd(nc, in_maps, core_ids=list(range(NCORE)))
    out = np.empty((B, S, D), np.float32)
    for c in range(NCORE):
        b, hf = c // 2, c % 2
        out[b, hf * TOWN:(hf + 1) * TOWN, :] = res.results[c]["outT"].T
    return out
